# version 01
# baseline (speedup 1.0000x reference)
import math
from contextlib import ExitStack

import numpy as np
import ml_dtypes

import concourse.bass as bass
import concourse.mybir as mybir
from concourse.bass_utils import run_bass_kernel_spmd

F32 = mybir.dt.float32
BF16 = mybir.dt.bfloat16
I32 = mybir.dt.int32
AF = mybir.ActivationFunctionType
ALU = mybir.AluOpType
AX = mybir.AxisListType

B, S, D, H, Dh = 4, 4096, 2048, 16, 128
HPC = 8
SQ = 512
NQT = S // SQ
NDC = D // 128
NT = S // 128
OWN = 2048
NTC = OWN // 128
I_EXP = 1408
NIC = I_EXP // 128
NEXP = 16
NPASS = 18
SCALE = 1.0 / math.sqrt(Dh)
RANK1_SLOTS = 3
T_SKIP = 150.0
INV_D = 1.0 / D
EPS = 1e-5


def _alibi_slopes(n):
    def pow2_slopes(n):
        start = 2 ** (-(2 ** (-(math.log2(n) - 3))))
        return [start * start ** i for i in range(n)]
    if math.log2(n).is_integer():
        return pow2_slopes(n)
    c = 2 ** math.floor(math.log2(n))
    return pow2_slopes(c) + _alibi_slopes(2 * c)[::2][: n - c]


SLOPES = _alibi_slopes(H)


def _chunk_start(j, qt):
    sl = SLOPES[2 * j + 1]
    nch = math.ceil((T_SKIP / sl + SQ) / 128.0)
    return max(0, 4 * (qt + 1) - nch)


class Chan:
    def __init__(self, stk, nc, name):
        self.sem = stk.enter_context(nc.semaphore(name))
        self.n = 0

    def inc(self, instr, k=1):
        instr.then_inc(self.sem, k)
        self.n += k
        return self.n

    def wait(self, eng, val=None):
        v = self.n if val is None else val
        if v > 0:
            eng.wait_ge(self.sem, v)


MAX_PHASE = 4
NT_LIM = NT
P1_GATE = True
P1_PROJ = True


def build_program(taps=()):
    nc = bass.Bass("TRN2", target_bir_lowering=False)

    x_full = nc.declare_dram_parameter("x_full", [S, D], F32, isOutput=False)
    x_own = nc.declare_dram_parameter("x_own", [OWN, D], F32, isOutput=False)
    wqT = nc.declare_dram_parameter("wqT", [128, NDC, HPC * 128], F32, isOutput=False)
    wkT = nc.declare_dram_parameter("wkT", [128, NDC, HPC * 128], F32, isOutput=False)
    wvT = nc.declare_dram_parameter("wvT", [128, NDC, HPC * 128], F32, isOutput=False)
    woT = nc.declare_dram_parameter("woT", [HPC, 128, D], F32, isOutput=False)
    bqk = nc.declare_dram_parameter("bqk", [128, 2, HPC], F32, isOutput=False)
    bvrow = nc.declare_dram_parameter("bvrow", [1, HPC * 128], F32, isOutput=False)
    borow = nc.declare_dram_parameter("borow", [1, D], F32, isOutput=False)
    gwT = nc.declare_dram_parameter("gwT", [128, NDC, 16], F32, isOutput=False)
    alibi = nc.declare_dram_parameter("alibi", [128, HPC, NQT, 32], F32, isOutput=False)
    nsq = nc.declare_dram_parameter("nsq", [1, RANK1_SLOTS, S], F32, isOutput=False)
    dmask = nc.declare_dram_parameter("dmask", [128, 4 * SQ], F32, isOutput=False)
    ident = nc.declare_dram_parameter("ident", [128, 128], F32, isOutput=False)
    meta = nc.declare_dram_parameter("meta", [1, 2], I32, isOutput=False)
    w1s = nc.declare_dram_parameter("w1s", [NEXP, NIC, 128, NDC, 128], BF16, isOutput=False)
    w3s = nc.declare_dram_parameter("w3s", [NEXP, NIC, 128, NDC, 128], BF16, isOutput=False)
    w2T = nc.declare_dram_parameter("w2T", [NEXP, I_EXP, D], BF16, isOutput=False)
    sw1s = nc.declare_dram_parameter("sw1s", [2 * NIC, 128, NDC, 128], BF16, isOutput=False)
    sw3s = nc.declare_dram_parameter("sw3s", [2 * NIC, 128, NDC, 128], BF16, isOutput=False)
    sw2T = nc.declare_dram_parameter("sw2T", [2 * I_EXP, D], BF16, isOutput=False)
    out_own = nc.declare_dram_parameter("out_own", [OWN, D], F32, isOutput=True)

    xnT_s = nc.dram_tensor("xnT_s", [NDC, 128, S], BF16)
    kt_s = nc.dram_tensor("kt_s", [HPC, 128, S], BF16)
    qt_s = nc.dram_tensor("qt_s", [HPC, 128, S], BF16)
    v_s = nc.dram_tensor("v_s", [HPC, 128, NT, 128], BF16)
    dw_s = nc.dram_tensor("dw_s", [128, NT, 16], F32)
    s_in = nc.dram_tensor("s_in", [S, D], F32)
    s_own = nc.dram_tensor("s_own", [OWN, D], F32)
    y_s = nc.dram_tensor("y_s", [OWN, D], F32)

    tap_map = {"xnT_s": xnT_s, "kt_s": kt_s, "qt_s": qt_s, "v_s": v_s,
               "dw_s": dw_s, "s_in": s_in, "s_own": s_own, "y_s": y_s}
    tap_params = []
    for tname in taps:
        tt = tap_map[tname]
        tap_params.append(
            (nc.declare_dram_parameter("tap_" + tname, list(tt.shape), tt.dtype,
                                       isOutput=True), tt))

    stk = ExitStack()
    E = stk.enter_context

    def sb(name, shape, dt):
        return E(nc.sbuf_tensor(name, shape, dt))

    ch = {}
    for name in [
        "c_ld", "c_cast", "c_ms", "bar",
        "p1v", "p1s", "p1tr", "p1ve", "p1se", "p1z", "p1g", "p1ge",
        "p1q", "p1qe",
        "p2sc", "p2vm", "p2exp", "p2ot", "p2l", "p2rl", "p2lbc",
        "p2oe", "p2ao", "p2pg", "p2sev",
        "cc",
        "p3x", "p3w2", "p3a", "p3sa", "p3h", "p3b", "p3ye",
        "p4g", "p4m",
    ]:
        ch[name] = Chan(stk, nc, "ch_" + name)
    for name, k in [("p1x", 2), ("p1dw", 2), ("p1xs", 2), ("p1qs", 4),
                    ("p2ld", 2), ("p2nq", 2), ("p2ss", 2), ("p3w13", 2),
                    ("p3yd", NTC), ("p4ld", 2), ("out", 2)]:
        ch[name] = [Chan(stk, nc, f"ch_{name}_{i}") for i in range(k)]
    def w13_cnt(pp, ic):
        if ic % 2 == 0:
            return 32 * (pp * 6 + ic // 2 + 1)
        return 32 * (pp * 5 + (ic - 1) // 2 + 1)

    qs_cnt = {}
    _run = [0, 0, 0, 0]
    for G in range(NQT):
        for i in range(24):
            _run[i % 4] += 16 * (1 if i < 16 else 4)
            qs_cnt[(G, i)] = _run[i % 4]

    ident_sb = sb("ident_sb", [128, 128], F32)
    ones_col = sb("ones_col", [128, 1], BF16)
    ones_row = sb("ones_row", [1, 128], F32)
    ones_vec = sb("ones_vec", [128, 1], F32)
    gwT_sb = sb("gwT_sb", [128, NDC, 16], F32)
    bqk_sb = sb("bqk_sb", [128, 2, HPC], F32)
    bvr_sb = sb("bvr_sb", [1, HPC * 128], F32)
    meta_sb = sb("meta_sb", [1, 2], I32)
    psum = E(nc.psum_tensor("psum", [128, 4096], F32))

    def bank(i, width=512):
        return psum[:, 512 * i: 512 * i + width]

    NBAR = [0]

    def barrier(block):
        NBAR[0] += 1
        target = 5 * NBAR[0]
        for en in ("sync", "tensor", "vector", "scalar", "gpsimd"):
            def _emit(eng):
                ch["bar"].inc(eng.nop(), 1)
                eng.wait_ge(ch["bar"].sem, target)
            getattr(block, en)(_emit)
        assert ch["bar"].n == target

    def v_count(t, k):
        return 8 * t + k + 1

    def s_count(t, k):
        return 3 * t + k + 1

    def tr_count(t, dc):
        return 16 * t + dc + 1

    def g_count(t, k):
        return 16 * t + k + 1

    def q_count(G, i):
        return 24 * G + i + 1

    def dma_idx_before(G, i):
        tot = 24 * G + i - 4 + 1
        full_groups, rem = divmod(tot, 24)
        n = full_groups * 48
        n += min(rem, 16) * 1 + max(0, rem - 16) * 4
        return n

    slots = []
    for qt in range(NQT):
        for j in range(HPC):
            c0 = _chunk_start(j, qt)
            slots.append((qt, j, c0, list(range(c0, 4 * (qt + 1)))))
    CUM = [0]
    for (_, _, _, cks) in slots:
        CUM.append(CUM[-1] + len(cks))
    vm_cnt = {}
    diag_gidx = []
    for u, (qt, j, c0, cks) in enumerate(slots):
        for i, ck in enumerate(cks):
            gidx = CUM[u] + i
            if ck >= 4 * qt:
                diag_gidx.append(gidx)
                vm_cnt[gidx] = len(diag_gidx)

    with nc.Block() as block:
        p1 = ExitStack()
        E1 = p1.enter_context

        def sb1(name, shape, dt):
            return E1(nc.sbuf_tensor(name, shape, dt))

        wq_sb = sb1("wq_sb", [128, NDC, HPC * 128], BF16)
        wk_sb = sb1("wk_sb", [128, NDC, HPC * 128], BF16)
        wv_sb = sb1("wv_sb", [128, NDC, HPC * 128], BF16)
        x_sb = [sb1(f"x_sb{i}", [128, D], F32) for i in range(2)]
        sq_sb = sb1("sq_sb", [128, D], F32)
        xnf_sb = [sb1(f"xnf_sb{i}", [128, D], F32) for i in range(2)]
        xnTb_sb = [sb1(f"xnTb_sb{i}", [128, NDC, SQ], BF16) for i in range(2)]
        xnTf_sb = [sb1(f"xnTf_sb{i}", [128, NDC, 128], F32) for i in range(2)]
        qkvst_sb = [sb1(f"qkvst{i}", [128, 512], BF16) for i in range(4)]
        ln = {n: [sb1(f"{n}{i}", [128, 1], F32) for i in range(2)]
              for n in ["msum", "ssq", "meanv", "ssqn", "m2v", "var", "lnv",
                        "rstd", "mrs", "nmrs"]}
        gt = {n: [sb1(f"{n}{i}", [128, 16], F32) for i in range(2)]
              for n in ["g_z", "g_k1", "g_k2", "g_z2", "g_t2", "g_dw"]}
        gv = {n: [sb1(f"{n}{i}", [128, 1], F32) for i in range(2)]
              for n in ["g_m1", "g_m2", "g_e1", "g_e2", "g_v1", "g_v2",
                        "g_vs", "g_rv", "g_w1", "g_w2"]}

        @block.sync
        def _(sync):
            for src, dst in [
                (ident[:, :], ident_sb[:]),
                (gwT[:, :, :], gwT_sb[:]), (bqk[:, :, :], bqk_sb[:]),
                (bvrow[:, :], bvr_sb[:]), (meta[:, :], meta_sb[:]),
            ]:
                ch["c_ld"].inc(sync.dma_start(out=dst, in_=src), 16)
            ch["c_ld"].wait(sync)

        @block.gpsimd
        def _(gp):
            for src, dst in [
                (wqT.ap().rearrange("p a b -> p (a b)"),
                 wq_sb.ap().rearrange("p a b -> p (a b)")),
                (wkT.ap().rearrange("p a b -> p (a b)"),
                 wk_sb.ap().rearrange("p a b -> p (a b)")),
                (wvT.ap().rearrange("p a b -> p (a b)"),
                 wv_sb.ap().rearrange("p a b -> p (a b)")),
            ]:
                ch["c_cast"].inc(gp.dma_start(out=dst, in_=src), 16)
            ch["c_cast"].wait(gp)

        @block.vector
        def _(v):
            ch["c_ms"].inc(v.memset(ones_col[:], 1.0), 1)
            ch["c_ms"].inc(v.memset(ones_row[:], 1.0), 1)
            ch["c_ms"].inc(v.memset(ones_vec[:], 1.0), 1)
            ch["c_ms"].wait(v)

        barrier(block)

        if MAX_PHASE >= 1:
            @block.sync
            def _(sync):
                for t in range(NT_LIM):
                    if t >= 2:
                        ch["p1v"].wait(sync, v_count(t - 2, 1))
                        ch["p1s"].wait(sync, s_count(t - 2, 2))
                    ch["p1x"][t % 2].inc(
                        sync.dma_start(out=x_sb[t % 2][:],
                                       in_=x_full[t * 128:(t + 1) * 128, :]), 16)

            @block.vector
            def _(v):
                for t in range(NT_LIM):
                    p = t % 2
                    G = t // 4
                    ch["p1x"][t % 2].wait(v, 16 * (t // 2 + 1))
                    if t >= 2:
                        ch["p1tr"].wait(v, tr_count(t - 2, 15))
                    ch["p1v"].inc(v.reduce_sum(out=ln["msum"][p][:], in_=x_sb[p][:],
                                               axis=AX.X), 1)
                    ch["p1v"].inc(v.scalar_tensor_tensor(
                        out=sq_sb[:], in0=x_sb[p][:], scalar=1.0, in1=x_sb[p][:],
                        op0=ALU.mult, op1=ALU.mult, accum_out=ln["ssq"][p][:]), 1)
                    ch["p1v"].wait(v, v_count(t, 0))
                    ch["p1v"].inc(v.tensor_scalar(out=ln["meanv"][p][:],
                                                  in0=ln["msum"][p][:], scalar1=INV_D,
                                                  scalar2=None, op0=ALU.mult), 1)
                    ch["p1v"].wait(v, v_count(t, 1))
                    ch["p1v"].inc(v.tensor_scalar(out=ln["ssqn"][p][:],
                                                  in0=ln["ssq"][p][:], scalar1=INV_D,
                                                  scalar2=EPS, op0=ALU.mult,
                                                  op1=ALU.add), 1)
                    ch["p1v"].wait(v, v_count(t, 2))
                    ch["p1v"].inc(v.tensor_mul(out=ln["m2v"][p][:], in0=ln["meanv"][p][:],
                                               in1=ln["meanv"][p][:]), 1)
                    ch["p1v"].wait(v, v_count(t, 4))
                    ch["p1v"].inc(v.tensor_sub(out=ln["var"][p][:], in0=ln["ssqn"][p][:],
                                               in1=ln["m2v"][p][:]), 1)
                    ch["p1s"].wait(v, s_count(t, 1))
                    ch["p1v"].inc(v.tensor_mul(out=ln["mrs"][p][:], in0=ln["meanv"][p][:],
                                               in1=ln["rstd"][p][:]), 1)
                    ch["p1v"].wait(v, v_count(t, 6))
                    ch["p1v"].inc(v.tensor_scalar(out=ln["nmrs"][p][:], in0=ln["mrs"][p][:],
                                                  scalar1=-1.0, scalar2=None,
                                                  op0=ALU.mult), 1)
                    if t % 4 == 0 and G >= 2:
                        if P1_PROJ:
                            ch["p1q"].wait(v, 24 * (G - 1))
                        ch["p1xs"][G % 2].wait(v, 256 * (G // 2))
                    for dc in range(NDC):
                        ch["p1se"].wait(v, tr_count(t, dc))
                        ch["p1ve"].inc(v.tensor_copy(
                            out=xnTb_sb[G % 2][:, dc, (t % 4) * 128:(t % 4) * 128 + 128],
                            in_=xnTf_sb[p][:, dc, :]), 1)
                    if not P1_GATE:
                        continue
                    ch["p1z"].wait(v, t + 1)
                    if t >= 2:
                        ch["p1dw"][t % 2].wait(v, 16 * (t // 2))
                    ch["p1g"].inc(v.tensor_copy(out=gt["g_z"][p][:], in_=bank(4, 16)), 1)
                    ch["p1g"].wait(v, g_count(t, 0))
                    ch["p1g"].inc(v.reduce_max(out=gv["g_m1"][p][:], in_=gt["g_z"][p][:],
                                               axis=AX.X), 1)
                    ch["p1g"].wait(v, g_count(t, 1))
                    ch["p1g"].inc(v.tensor_scalar(out=gt["g_k1"][p][:], in0=gt["g_z"][p][:],
                                                  scalar1=gv["g_m1"][p][:], scalar2=None,
                                                  op0=ALU.is_ge), 1)
                    ch["p1g"].wait(v, g_count(t, 2))
                    ch["p1g"].inc(v.scalar_tensor_tensor(
                        out=gt["g_z2"][p][:], in0=gt["g_k1"][p][:], scalar=-1e30,
                        in1=gt["g_z"][p][:], op0=ALU.mult, op1=ALU.add), 1)
                    ch["p1g"].wait(v, g_count(t, 3))
                    ch["p1g"].inc(v.reduce_max(out=gv["g_m2"][p][:], in_=gt["g_z2"][p][:],
                                               axis=AX.X), 1)
                    ch["p1g"].wait(v, g_count(t, 4))
                    ch["p1g"].inc(v.tensor_scalar(out=gt["g_k2"][p][:], in0=gt["g_z2"][p][:],
                                                  scalar1=gv["g_m2"][p][:], scalar2=None,
                                                  op0=ALU.is_ge), 1)
                    ch["p1ge"].wait(v, 2 * t + 1)
                    ch["p1g"].inc(v.tensor_scalar(out=gv["g_v1"][p][:], in0=gv["g_e1"][p][:],
                                                  scalar1=1.0, scalar2=None, op0=ALU.add), 1)
                    ch["p1g"].wait(v, g_count(t, 6))
                    ch["p1g"].inc(v.reciprocal(out=gv["g_v1"][p][:], in_=gv["g_v1"][p][:]), 1)
                    ch["p1ge"].wait(v, 2 * t + 2)
                    ch["p1g"].inc(v.tensor_scalar(out=gv["g_v2"][p][:], in0=gv["g_e2"][p][:],
                                                  scalar1=1.0, scalar2=None, op0=ALU.add), 1)
                    ch["p1g"].wait(v, g_count(t, 8))
                    ch["p1g"].inc(v.reciprocal(out=gv["g_v2"][p][:], in_=gv["g_v2"][p][:]), 1)
                    ch["p1g"].wait(v, g_count(t, 9))
                    ch["p1g"].inc(v.tensor_add(out=gv["g_vs"][p][:], in0=gv["g_v1"][p][:],
                                               in1=gv["g_v2"][p][:]), 1)
                    ch["p1g"].wait(v, g_count(t, 10))
                    ch["p1g"].inc(v.reciprocal(out=gv["g_rv"][p][:], in_=gv["g_vs"][p][:]), 1)
                    ch["p1g"].wait(v, g_count(t, 11))
                    ch["p1g"].inc(v.tensor_mul(out=gv["g_w1"][p][:], in0=gv["g_v1"][p][:],
                                               in1=gv["g_rv"][p][:]), 1)
                    ch["p1g"].inc(v.tensor_mul(out=gv["g_w2"][p][:], in0=gv["g_v2"][p][:],
                                               in1=gv["g_rv"][p][:]), 1)
                    ch["p1g"].wait(v, g_count(t, 13))
                    ch["p1g"].inc(v.tensor_scalar(out=gt["g_t2"][p][:], in0=gt["g_k2"][p][:],
                                                  scalar1=gv["g_w2"][p][:], scalar2=None,
                                                  op0=ALU.mult), 1)
                    ch["p1g"].wait(v, g_count(t, 14))
                    ch["p1g"].inc(v.scalar_tensor_tensor(
                        out=gt["g_dw"][p][:], in0=gt["g_k1"][p][:], scalar=gv["g_w1"][p][:],
                        in1=gt["g_t2"][p][:], op0=ALU.mult, op1=ALU.add), 1)
                assert ch["p1v"].n == 8 * NT_LIM
                assert ch["p1ve"].n == 16 * NT_LIM

            @block.scalar
            def _(s):
                for t in range(NT_LIM):
                    p = t % 2
                    ch["p1v"].wait(s, v_count(t, 5))
                    ch["p1s"].inc(s.activation(out=ln["lnv"][p][:], in_=ln["var"][p][:],
                                               func=AF.Ln), 1)
                    ch["p1s"].wait(s, s_count(t, 0))
                    ch["p1s"].inc(s.activation(out=ln["rstd"][p][:], in_=ln["lnv"][p][:],
                                               func=AF.Exp, scale=-0.5), 1)
                    ch["p1x"][t % 2].wait(s, 16 * (t // 2 + 1))
                    ch["p1s"].wait(s, s_count(t, 1))
                    ch["p1v"].wait(s, v_count(t, 7))
                    if t >= 2:
                        ch["p1tr"].wait(s, tr_count(t - 2, 15))
                    ch["p1s"].inc(s.activation(out=xnf_sb[p][:], in_=x_sb[p][:],
                                               func=AF.Identity, scale=ln["rstd"][p][:],
                                               bias=ln["nmrs"][p][:]), 1)
                    if t >= 2:
                        if P1_GATE:
                            ch["p1z"].wait(s, t - 1)
                        ch["p1ve"].wait(s, 16 * (t - 1))
                    for dc in range(NDC):
                        ch["p1tr"].wait(s, tr_count(t, dc))
                        ch["p1se"].inc(s.copy(out=xnTf_sb[p][:, dc, :],
                                              in_=bank(dc % 2, 128)), 1)
                    if P1_GATE:
                        ch["p1g"].wait(s, g_count(t, 1))
                        ch["p1ge"].inc(s.activation(out=gv["g_e1"][p][:], in_=gv["g_m1"][p][:],
                                                    func=AF.Exp, scale=-1.0), 1)
                        ch["p1g"].wait(s, g_count(t, 4))
                        ch["p1ge"].inc(s.activation(out=gv["g_e2"][p][:], in_=gv["g_m2"][p][:],
                                                    func=AF.Exp, scale=-1.0), 1)
                    if P1_PROJ and t % 4 == 3:
                        G = t // 4
                        for i in range(24):
                            ch["p1q"].wait(s, q_count(G, i))
                            if 24 * G + i >= 4:
                                Gp, ip = divmod(24 * G + i - 4, 24)
                                ch["p1qs"][i % 4].wait(s, qs_cnt[(Gp, ip)])
                            st = qkvst_sb[i % 4]
                            if i < 8:
                                ch["p1qe"].inc(s.activation(
                                    out=st[:], in_=bank(2 + i % 2), func=AF.Identity,
                                    bias=bqk_sb[:, 0, i:i + 1]), 1)
                            elif i < 16:
                                ch["p1qe"].inc(s.activation(
                                    out=st[:], in_=bank(2 + i % 2), func=AF.Identity,
                                    bias=bqk_sb[:, 1, i - 8:i - 7]), 1)
                            else:
                                ch["p1qe"].inc(s.copy(out=st[:], in_=bank(2 + i % 2)), 1)
                assert ch["p1s"].n == 3 * NT_LIM and ch["p1se"].n == 16 * NT_LIM
                assert (not P1_PROJ) or ch["p1qe"].n == 24 * NQT

            @block.tensor
            def _(te):
                for t in range(NT_LIM):
                    p = t % 2
                    G = t // 4
                    ch["p1s"].wait(te, s_count(t, 2))
                    for dc in range(NDC):
                        if 16 * t + dc >= 2:
                            ch["p1se"].wait(te, 16 * t + dc - 1)
                        ch["p1tr"].inc(te.transpose(
                            bank(dc % 2, 128), xnf_sb[p][:, dc * 128:(dc + 1) * 128],
                            ident_sb[:]), 1)
                    if P1_GATE:
                        ch["p1se"].wait(te, 16 * (t + 1))
                        if t >= 1:
                            ch["p1g"].wait(te, g_count(t - 1, 0))
                        mm = None
                        for dc in range(NDC):
                            mm = te.matmul(bank(4, 16), xnTf_sb[p][:, dc, :],
                                           gwT_sb[:, dc, :],
                                           start=(dc == 0), stop=(dc == NDC - 1))
                        ch["p1z"].inc(mm, 1)
                    if P1_PROJ and t % 4 == 3:
                        ch["p1ve"].wait(te, 64 * (G + 1))
                        for i in range(24):
                            bk = 2 + i % 2
                            if 24 * G + i >= 2:
                                ch["p1qe"].wait(te, q_count(G, i) - 2)
                            if i < 8:
                                for dc in range(NDC):
                                    mm = te.matmul(bank(bk),
                                                   wq_sb[:, dc, i * 128:(i + 1) * 128],
                                                   xnTb_sb[G % 2][:, dc, :],
                                                   start=(dc == 0), stop=(dc == NDC - 1))
                                ch["p1q"].inc(mm, 1)
                            elif i < 16:
                                j = i - 8
                                for dc in range(NDC):
                                    mm = te.matmul(bank(bk),
                                                   wk_sb[:, dc, j * 128:(j + 1) * 128],
                                                   xnTb_sb[G % 2][:, dc, :],
                                                   start=(dc == 0), stop=(dc == NDC - 1))
                                ch["p1q"].inc(mm, 1)
                            else:
                                vi = i - 16
                                fh, tc = divmod(vi, 4)
                                for dc in range(NDC):
                                    te.matmul(bank(bk),
                                              xnTb_sb[G % 2][:, dc, tc * 128:(tc + 1) * 128],
                                              wv_sb[:, dc, fh * 512:(fh + 1) * 512],
                                              start=(dc == 0), stop=False)
                                mm = te.matmul(bank(bk), ones_row[:],
                                               bvr_sb[0:1, fh * 512:(fh + 1) * 512],
                                               start=False, stop=True)
                                ch["p1q"].inc(mm, 1)
                assert ch["p1tr"].n == 16 * NT_LIM

            @block.gpsimd
            def _(gp):
                for t in range(NT_LIM):
                    p = t % 2
                    if P1_GATE:
                        ch["p1g"].wait(gp, g_count(t, 15))
                        ch["p1dw"][t % 2].inc(gp.dma_start(out=dw_s[:, t, :],
                                                           in_=gt["g_dw"][p][:]), 16)
                    if t % 4 == 3:
                        G = t // 4
                        ch["p1ve"].wait(gp, 64 * (G + 1))
                        for dc in range(NDC):
                            ch["p1xs"][G % 2].inc(gp.dma_start(
                                out=xnT_s[dc, :, G * 512:(G + 1) * 512],
                                in_=xnTb_sb[G % 2][:, dc, :]), 16)
                        for i in range(24 if P1_PROJ else 0):
                            ch["p1qe"].wait(gp, q_count(G, i))
                            st = qkvst_sb[i % 4]
                            if i < 8:
                                ch["p1qs"][i % 4].inc(gp.dma_start(
                                    out=qt_s[i, :, G * 512:(G + 1) * 512], in_=st[:]), 16)
                            elif i < 16:
                                ch["p1qs"][i % 4].inc(gp.dma_start(
                                    out=kt_s[i - 8, :, G * 512:(G + 1) * 512], in_=st[:]), 16)
                            else:
                                vi = i - 16
                                fh, tc = divmod(vi, 4)
                                for jj in range(4):
                                    j = fh * 4 + jj
                                    ch["p1qs"][i % 4].inc(gp.dma_start(
                                        out=v_s[j, :, G * 4 + tc, :],
                                        in_=st[:, jj * 128:(jj + 1) * 128]), 16)
                            assert ch["p1qs"][i % 4].n == qs_cnt[(G, i)]
                for c_ in ch["p1dw"] + ch["p1qs"] + ch["p1xs"]:
                    c_.wait(gp)

            barrier(block)
            p1.close()

        if MAX_PHASE >= 2:
            p2 = ExitStack()
            E2b = p2.enter_context

            def sbA(name, shape, dt):
                return E2b(nc.sbuf_tensor(name, shape, dt))

            wo_sb = sbA("wo_sb", [128, HPC, D], BF16)
            alibi_sb = sbA("alibi_sb", [128, HPC, NQT, 32], F32)
            dmask_sb = sbA("dmask_sb", [128, 4 * SQ], F32)
            bo_sb = sbA("bo_sb", [1, D], F32)
            nsq_sb = [sbA(f"nsq_sb{i}", [1, RANK1_SLOTS * 512], F32) for i in range(2)]
            kt_sb = [sbA(f"kt_sb{i}", [128, 32 * 128], BF16) for i in range(2)]
            vt_sb = [sbA(f"vt_sb{i}", [128, 32, 128], BF16) for i in range(2)]
            qtile_sb = [sbA(f"qtile_sb{i}", [128, 512], BF16) for i in range(2)]
            pt_sb = [sbA(f"pt_sb{i}", [128, 512], BF16) for i in range(4)]
            tdiag_sb = [sbA(f"tdiag_sb{i}", [128, 512], F32) for i in range(2)]
            otf_sb = [sbA(f"otf_sb{i}", [128, 512], F32) for i in range(2)]
            rl_sb = [sbA(f"rl_sb{i}", [1, 512], F32) for i in range(2)]
            ao_sb = [sbA(f"ao_sb{i}", [128, HPC, 512], BF16) for i in range(2)]
            sst_sb = [sbA(f"sst_sb{i}", [128, D], F32) for i in range(2)]

            @block.gpsimd
            def _(gp):
                ch["c_cast"].inc(gp.dma_start(
                    out=wo_sb[:, :, :],
                    in_=woT.ap().rearrange("a p b -> p a b")), 16)

            @block.sync
            def _(sync):
                for c_ in ch["p1qs"]:
                    c_.wait(sync)
                for srcv, dstv in [(alibi[:, :, :, :], alibi_sb[:]),
                                   (dmask[:, :], dmask_sb[:]),
                                   (borow[:, :], bo_sb[:])]:
                    ch["c_ld"].inc(sync.dma_start(out=dstv, in_=srcv), 16)
                for u, (qt, j, c0, cks) in enumerate(slots):
                    if j == 0:
                        if qt >= 2:
                            ch["p2sc"].wait(sync, CUM[(qt - 2) * 8 + RANK1_SLOTS])
                        ch["p2nq"][qt % 2].inc(sync.dma_start(
                            out=nsq_sb[qt % 2].ap().rearrange("o (a b) -> o a b", a=RANK1_SLOTS),
                            in_=nsq[:, :, qt * 512:(qt + 1) * 512]), 16)
                    W = len(cks)
                    if u >= 2:
                        ch["p2ot"].wait(sync, CUM[u - 1])
                    ch["p2ld"][u % 2].inc(sync.dma_start(
                        out=kt_sb[u % 2][:, 0:W * 128],
                        in_=kt_s[j, :, c0 * 128:(c0 + W) * 128]), 16)
                    ch["p2ld"][u % 2].inc(sync.dma_start(
                        out=vt_sb[u % 2][:, 0:W, :],
                        in_=v_s[j, :, c0:c0 + W, :]), 16)
                    ch["p2ld"][u % 2].inc(sync.dma_start(
                        out=qtile_sb[u % 2][:],
                        in_=qt_s[j, :, qt * 512:(qt + 1) * 512]), 16)
                    if j == HPC - 1:
                        for tc in range(4):
                            ch["p2sev"].wait(sync, 16 * qt + 4 * tc + 4)
                            ch["p2ss"][tc % 2].inc(sync.dma_start(
                                out=s_in[qt * 512 + tc * 128: qt * 512 + (tc + 1) * 128, :],
                                in_=sst_sb[tc % 2][:]), 16)

            @block.tensor
            def _(te):
                ch["c_cast"].wait(te, 64)
                ch["c_ld"].wait(te)
                for u, (qt, j, c0, cks) in enumerate(slots):
                    rank1 = j < RANK1_SLOTS
                    ch["p2ld"][u % 2].wait(te, 48 * (u // 2 + 1))
                    if rank1 and j == 0:
                        ch["p2nq"][qt % 2].wait(te, 16 * (qt // 2 + 1))
                    if u >= 1:
                        ch["p2oe"].wait(te, u)
                        ch["p2rl"].wait(te, u)
                    for i, ck in enumerate(cks):
                        gidx = CUM[u] + i
                        bk = gidx % 2
                        if gidx >= 2:
                            prev = gidx - 2
                            if prev in vm_cnt:
                                ch["p2vm"].wait(te, vm_cnt[prev])
                            else:
                                ch["p2exp"].wait(te, prev + 1)
                        mm = te.matmul(bank(bk), kt_sb[u % 2][:, i * 128:(i + 1) * 128],
                                       qtile_sb[u % 2][:], start=True, stop=not rank1)
                        if rank1:
                            mm = te.matmul(bank(bk), ones_row[:],
                                           nsq_sb[qt % 2][0:1, j * 512:(j + 1) * 512],
                                           start=False, stop=True)
                        ch["p2sc"].inc(mm, 1)
                        ch["p2exp"].wait(te, gidx + 1)
                        mm = te.matmul(psum[0:1, 3 * 512:4 * 512], ones_col[:],
                                       pt_sb[gidx % 4][:],
                                       start=(i == 0), stop=(i == len(cks) - 1))
                        if i == len(cks) - 1:
                            ch["p2l"].inc(mm, 1)
                        ch["p2ot"].inc(te.matmul(bank(2), vt_sb[u % 2][:, i, :],
                                                 pt_sb[gidx % 4][:],
                                                 start=(i == 0),
                                                 stop=(i == len(cks) - 1)), 1)
                    ch["p2rl"].wait(te, u + 1)
                    if u >= 1:
                        ch["p2ao"].wait(te, u)
                    ch["p2lbc"].inc(te.matmul(bank(4), ones_row[:], rl_sb[u % 2][:],
                                              start=True, stop=True), 1)
                    if j == HPC - 1:
                        ch["p2ao"].wait(te, 8 * (qt + 1))
                        for grp in range(16):
                            tc, fc = divmod(grp, 4)
                            bk = 5 + grp % 3
                            if 16 * qt + grp >= 3:
                                ch["p2sev"].wait(te, 16 * qt + grp - 2)
                            for jj in range(HPC):
                                te.matmul(bank(bk),
                                          ao_sb[qt % 2][:, jj, tc * 128:(tc + 1) * 128],
                                          wo_sb[:, jj, fc * 512:(fc + 1) * 512],
                                          start=(jj == 0), stop=False)
                            mm = te.matmul(bank(bk), ones_row[:],
                                           bo_sb[0:1, fc * 512:(fc + 1) * 512],
                                           start=False, stop=True)
                            ch["p2pg"].inc(mm, 1)
                assert ch["p2sc"].n == CUM[-1] and ch["p2pg"].n == 16 * NQT

            @block.vector
            def _(v):
                ch["c_ld"].wait(v)
                for u, (qt, j, c0, cks) in enumerate(slots):
                    for i, ck in enumerate(cks):
                        gidx = CUM[u] + i
                        if ck >= 4 * qt:
                            m = ck - 4 * qt
                            n = vm_cnt[gidx]
                            ch["p2sc"].wait(v, gidx + 1)
                            if n > 2:
                                ch["p2exp"].wait(v, diag_gidx[n - 3] + 1)
                            ch["p2vm"].inc(v.scalar_tensor_tensor(
                                out=tdiag_sb[n % 2][:], in0=bank(gidx % 2),
                                scalar=1.0, in1=dmask_sb[:, m * 512:(m + 1) * 512],
                                op0=ALU.mult, op1=ALU.add), 1)
                    ch["p2l"].wait(v, u + 1)
                    if u >= 2:
                        ch["p2lbc"].wait(v, u - 1)
                    ch["p2rl"].inc(v.reciprocal(out=rl_sb[u % 2][:],
                                                in_=psum[0:1, 3 * 512:4 * 512]), 1)
                    ch["p2oe"].wait(v, u + 1)
                    ch["p2lbc"].wait(v, u + 1)
                    if j == 0 and qt >= 2:
                        ch["p2sev"].wait(v, 16 * (qt - 1))
                    ch["p2ao"].inc(v.tensor_mul(out=ao_sb[qt % 2][:, j, :],
                                                in0=otf_sb[u % 2][:], in1=bank(4)), 1)
                    if j == HPC - 1:
                        for grp in range(16):
                            tc, fc = divmod(grp, 4)
                            ch["p2pg"].wait(v, 16 * qt + grp + 1)
                            if 4 * qt + tc >= 2:
                                ch["p2ss"][tc % 2].wait(v, 16 * ((4 * qt + tc) // 2))
                            ch["p2sev"].inc(v.tensor_copy(
                                out=sst_sb[tc % 2][:, fc * 512:(fc + 1) * 512],
                                in_=bank(5 + grp % 3)), 1)
                assert ch["p2vm"].n == len(diag_gidx) and ch["p2sev"].n == 16 * NQT

            @block.scalar
            def _(s):
                ch["c_ld"].wait(s)
                for u, (qt, j, c0, cks) in enumerate(slots):
                    for i, ck in enumerate(cks):
                        gidx = CUM[u] + i
                        if gidx >= 4:
                            ch["p2ot"].wait(s, gidx - 3)
                        bias_ap = alibi_sb[:, j, qt, ck:ck + 1]
                        if ck >= 4 * qt:
                            ch["p2vm"].wait(s, vm_cnt[gidx])
                            src = tdiag_sb[vm_cnt[gidx] % 2][:]
                        else:
                            ch["p2sc"].wait(s, gidx + 1)
                            src = bank(gidx % 2)
                        ch["p2exp"].inc(s.activation(out=pt_sb[gidx % 4][:], in_=src,
                                                     func=AF.Exp, bias=bias_ap,
                                                     scale=SCALE), 1)
                    ch["p2ot"].wait(s, CUM[u + 1])
                    if u >= 2:
                        ch["p2ao"].wait(s, u - 1)
                    ch["p2oe"].inc(s.copy(out=otf_sb[u % 2][:], in_=bank(2)), 1)
                assert ch["p2exp"].n == CUM[-1]

            barrier(block)

        if MAX_PHASE >= 2.5:
            @block.gpsimd
            def _(gp):
                for c_ in ch["p2ss"]:
                    c_.wait(gp)
                ch["cc"].inc(gp.collective_compute(
                    "ReduceScatter", ALU.add,
                    replica_groups=[[0, 1], [2, 3], [4, 5], [6, 7]],
                    ins=[s_in.ap().opt()], outs=[s_own.ap().opt()]), 1)

            p2.close()

        if MAX_PHASE >= 3:
            p3 = ExitStack()
            E3 = p3.enter_context

            def sb3(name, shape, dt):
                return E3(nc.sbuf_tensor(name, shape, dt))

            xnT_own = sb3("xnT_own", [128, NDC, OWN], BF16)
            w13_sb = [sb3(f"w13_sb{i}", [128, NDC, 128], BF16) for i in range(4)]
            w2_sb = sb3("w2_sb", [128, NIC, D], BF16)
            h_sb = sb3("h_sb", [128, NIC, OWN], BF16)
            sa_sb = [sb3(f"sa_sb{i}", [128, 512], F32) for i in range(2)]
            ysc_sb = [sb3(f"ysc_sb{i}", [128, D], F32) for i in range(2)]
            dw_sb = sb3("dw_sb", [128, NTC, 16], F32)

            def wslab(pp, ic, which):
                if pp < NEXP:
                    return (w1s if which == 0 else w3s)[pp, ic, :, :, :]
                off = (pp - NEXP) * NIC + ic
                return (sw1s if which == 0 else sw3s)[off, :, :, :]

            def w2slab(pp, ic):
                if pp < NEXP:
                    return w2T[pp, ic * 128:(ic + 1) * 128, :]
                off = (pp - NEXP) * I_EXP + ic * 128
                return sw2T[off: off + 128, :]

            @block.gpsimd
            def _(gp):
                goff = nc.values_load(meta_sb[0:1, 0:1], engines=[mybir.EngineType.Pool])
                gtile = nc.values_load(meta_sb[0:1, 1:2], engines=[mybir.EngineType.Pool])
                for dc in range(NDC):
                    ch["p3x"].inc(gp.dma_start(
                        out=xnT_own[:, dc, :],
                        in_=xnT_s[dc, :, bass.ds(goff, OWN)]), 16)
                ch["p3x"].inc(gp.dma_start(
                    out=dw_sb[:, :, :], in_=dw_s[:, bass.ds(gtile, NTC), :]), 16)
                for pp in range(NPASS):
                    for tc in range(NTC):
                        ch["p3ye"].wait(gp, pp * 64 + 4 * tc + 4)
                        if pp >= 1:
                            ch["p3yd"][tc].wait(gp, 16 * pp)
                        dst = y_s[tc * 128:(tc + 1) * 128, :]
                        if pp == 0:
                            dma = gp.dma_start(out=dst, in_=ysc_sb[tc % 2][:])
                        else:
                            dma = gp.dma_start(out=dst, in_=ysc_sb[tc % 2][:],
                                               accum_op=ALU.add)
                        ch["p3yd"][tc].inc(dma, 16)

            @block.sync
            def _(sync):
                for c_ in ch["p2ss"]:
                    c_.wait(sync)
                for pp in range(NPASS):
                    for ic in range(NIC):
                        k = pp * NIC + ic
                        kprev = k - 2 if ic >= 1 else k - 1
                        if kprev >= 0 and k >= 2:
                            ch["p3a"].wait(sync, 8 * (kprev + 1))
                        ch["p3w13"][ic % 2].inc(sync.dma_start(
                            out=w13_sb[2 * (ic % 2)][:], in_=wslab(pp, ic, 0)), 16)
                        ch["p3w13"][ic % 2].inc(sync.dma_start(
                            out=w13_sb[2 * (ic % 2) + 1][:], in_=wslab(pp, ic, 1)), 16)
                    if pp >= 1:
                        ch["p3b"].wait(sync, 64 * pp)
                    for ic in range(NIC):
                        ch["p3w2"].inc(sync.dma_start(
                            out=w2_sb[:, ic, :], in_=w2slab(pp, ic)), 16)

            @block.tensor
            def _(te):
                ch["p3x"].wait(te)
                for pp in range(NPASS):
                    for ic in range(NIC):
                        ch["p3w13"][ic % 2].wait(te, w13_cnt(pp, ic))
                        mm = None
                        for tg in range(4):
                            kk = ic * 4 + tg
                            if pp * 44 + kk >= 2:
                                ch["p3sa"].wait(te, pp * 44 + kk - 1)
                                ch["p3h"].wait(te, pp * 44 + kk - 1)
                            for dc in range(NDC):
                                mm = te.matmul(bank(kk % 2), w13_sb[2 * (ic % 2)][:, dc, :],
                                               xnT_own[:, dc, tg * 512:(tg + 1) * 512],
                                               start=(dc == 0), stop=(dc == NDC - 1))
                            ch["p3a"].inc(mm, 1)
                            for dc in range(NDC):
                                mm = te.matmul(bank(2 + kk % 2),
                                               w13_sb[2 * (ic % 2) + 1][:, dc, :],
                                               xnT_own[:, dc, tg * 512:(tg + 1) * 512],
                                               start=(dc == 0), stop=(dc == NDC - 1))
                            ch["p3a"].inc(mm, 1)
                    ch["p3h"].wait(te, 44 * (pp + 1))
                    ch["p3w2"].wait(te, 176 * (pp + 1))
                    for tc in range(NTC):
                        for fc in range(4):
                            if pp * 64 + 4 * tc + fc >= 4:
                                ch["p3ye"].wait(te, pp * 64 + 4 * tc + fc - 3)
                            mm = None
                            for ic in range(NIC):
                                mm = te.matmul(bank(4 + fc),
                                               h_sb[:, ic, tc * 128:(tc + 1) * 128],
                                               w2_sb[:, ic, fc * 512:(fc + 1) * 512],
                                               start=(ic == 0), stop=(ic == NIC - 1))
                            ch["p3b"].inc(mm, 1)
                assert ch["p3a"].n == 88 * NPASS and ch["p3b"].n == 64 * NPASS

            @block.scalar
            def _(s):
                ch["p3x"].wait(s)
                for pp in range(NPASS):
                    for ic in range(NIC):
                        for tg in range(4):
                            kk = ic * 4 + tg
                            ch["p3a"].wait(s, pp * 88 + 2 * kk + 1)
                            if pp * 44 + kk >= 2:
                                ch["p3h"].wait(s, pp * 44 + kk - 1)
                            ch["p3sa"].inc(s.activation(out=sa_sb[kk % 2][:],
                                                        in_=bank(kk % 2),
                                                        func=AF.Silu), 1)
                    for tc in range(NTC):
                        for fc in range(4):
                            ch["p3b"].wait(s, pp * 64 + 4 * tc + fc + 1)
                            if pp * 16 + tc >= 2:
                                if tc >= 2:
                                    ch["p3yd"][tc - 2].wait(s, 16 * (pp + 1))
                                else:
                                    ch["p3yd"][tc + 14].wait(s, 16 * pp)
                            scale_ap = (dw_sb[:, tc, pp:pp + 1] if pp < NEXP
                                        else ones_vec[:])
                            ch["p3ye"].inc(s.activation(
                                out=ysc_sb[tc % 2][:, fc * 512:(fc + 1) * 512],
                                in_=bank(4 + fc), func=AF.Identity, scale=scale_ap), 1)
                assert ch["p3sa"].n == 44 * NPASS and ch["p3ye"].n == 64 * NPASS

            @block.vector
            def _(v):
                for pp in range(NPASS):
                    if pp >= 1:
                        ch["p3b"].wait(v, 64 * pp)
                    for ic in range(NIC):
                        for tg in range(4):
                            kk = ic * 4 + tg
                            ch["p3sa"].wait(v, pp * 44 + kk + 1)
                            ch["p3a"].wait(v, pp * 88 + 2 * kk + 2)
                            ch["p3h"].inc(v.tensor_mul(
                                out=h_sb[:, ic, tg * 512:(tg + 1) * 512],
                                in0=sa_sb[kk % 2][:], in1=bank(2 + kk % 2)), 1)
                assert ch["p3h"].n == 44 * NPASS

            barrier(block)
            p3.close()

        p4 = ExitStack()
        E4 = p4.enter_context
        y4 = [E4(nc.sbuf_tensor(f"y4_{i}", [128, D], F32)) for i in range(2)]
        s4 = [E4(nc.sbuf_tensor(f"s4_{i}", [128, D], F32)) for i in range(2)]
        x4 = [E4(nc.sbuf_tensor(f"x4_{i}", [128, D], F32)) for i in range(2)]
        g4 = [E4(nc.sbuf_tensor(f"g4_{i}", [128, D], F32)) for i in range(2)]
        o4 = [E4(nc.sbuf_tensor(f"o4_{i}", [128, D], F32)) for i in range(2)]

        @block.sync
        def _(sync):
            for c_ in ch["p3yd"]:
                c_.wait(sync)
            ch["cc"].wait(sync, 1)
            for tc in range(NTC):
                p = tc % 2
                if tc >= 2:
                    ch["p4g"].wait(sync, tc - 1)
                    ch["p4m"].wait(sync, 2 * (tc - 1))
                ch["p4ld"][p].inc(sync.dma_start(
                    out=y4[p][:], in_=y_s[tc * 128:(tc + 1) * 128, :]), 16)
                ch["p4ld"][p].inc(sync.dma_start(
                    out=s4[p][:], in_=s_own[tc * 128:(tc + 1) * 128, :]), 16)
                ch["p4ld"][p].inc(sync.dma_start(
                    out=x4[p][:], in_=x_own[tc * 128:(tc + 1) * 128, :]), 16)

        @block.scalar
        def _(s):
            for tc in range(NTC):
                p = tc % 2
                ch["p4ld"][p].wait(s, 48 * (tc // 2 + 1))
                if tc >= 2:
                    ch["p4m"].wait(s, 2 * (tc - 1))
                ch["p4g"].inc(s.activation(out=g4[p][:], in_=y4[p][:],
                                           func=AF.Silu), 1)

        @block.vector
        def _(v):
            for tc in range(NTC):
                p = tc % 2
                ch["p4g"].wait(v, tc + 1)
                ch["p4ld"][p].wait(v, 48 * (tc // 2 + 1))
                if tc >= 2:
                    ch["out"][tc % 2].wait(v, 16 * (tc // 2))
                ch["p4m"].inc(v.tensor_mul(out=o4[p][:], in0=s4[p][:],
                                           in1=g4[p][:]), 1)
                ch["p4m"].wait(v, 2 * tc + 1)
                ch["p4m"].inc(v.tensor_add(out=o4[p][:], in0=o4[p][:],
                                           in1=x4[p][:]), 1)

        @block.gpsimd
        def _(gp):
            for tc in range(NTC):
                p = tc % 2
                ch["p4m"].wait(gp, 2 * (tc + 1))
                ch["out"][p].inc(gp.dma_start(
                    out=out_own[tc * 128:(tc + 1) * 128, :], in_=o4[p][:]), 16)
            ch["out"][0].wait(gp)
            ch["out"][1].wait(gp)
            for tp, tt in tap_params:
                ch["out"][0].inc(gp.dma_start(out=tp.ap(), in_=tt.ap()), 16)
            ch["out"][0].wait(gp)

        p4.close()

    stk.close()
    return nc



def _prep_inputs(inputs):
    f32 = np.float32
    x = np.ascontiguousarray(np.asarray(inputs["L0_attn_x"], dtype=f32))
    Wq = np.asarray(inputs["L0_attn_Wq"], dtype=f32)
    Wk = np.asarray(inputs["L0_attn_Wk"], dtype=f32)
    Wv = np.asarray(inputs["L0_attn_Wv"], dtype=f32)
    Wo = np.asarray(inputs["L0_attn_Wo"], dtype=f32)
    bq = np.asarray(inputs["L0_attn_bq"], dtype=f32)
    bk = np.asarray(inputs["L0_attn_bk"], dtype=f32)
    bv = np.asarray(inputs["L0_attn_bv"], dtype=f32)
    bo = np.asarray(inputs["L0_attn_bo"], dtype=f32)
    gate_w = np.asarray(inputs["L0_ffn_gate_w"], dtype=f32)
    w1 = np.asarray(inputs["L0_ffn_w1"])
    w2 = np.asarray(inputs["L0_ffn_w2"])
    w3 = np.asarray(inputs["L0_ffn_w3"])
    sw1 = np.asarray(inputs["L0_ffn_sw1"])
    sw2 = np.asarray(inputs["L0_ffn_sw2"])
    sw3 = np.asarray(inputs["L0_ffn_sw3"])

    w1s = np.ascontiguousarray(
        w1.transpose(0, 2, 1).reshape(NEXP, NDC, 128, NIC, 128).transpose(0, 3, 2, 1, 4))
    w3s = np.ascontiguousarray(
        w3.transpose(0, 2, 1).reshape(NEXP, NDC, 128, NIC, 128).transpose(0, 3, 2, 1, 4))
    w2Ta = np.ascontiguousarray(w2.transpose(0, 2, 1))
    sw1s = np.ascontiguousarray(
        sw1.T.reshape(NDC, 128, 2 * NIC, 128).transpose(2, 1, 0, 3))
    sw3s = np.ascontiguousarray(
        sw3.T.reshape(NDC, 128, 2 * NIC, 128).transpose(2, 1, 0, 3))
    sw2Ta = np.ascontiguousarray(sw2.T)
    gwTa = np.ascontiguousarray(gate_w.T.reshape(NDC, 128, 16).transpose(1, 0, 2))
    identa = np.eye(128, dtype=f32)

    dmaska = np.zeros((128, 4, SQ), dtype=f32)
    kl = np.arange(128)[:, None]
    ql = np.arange(SQ)[None, :]
    for m in range(4):
        dmaska[:, m, :] = np.where(m * 128 + kl <= ql, 0.0, -1e30)
    dmaska = np.ascontiguousarray(dmaska.reshape(128, 4 * SQ))

    in_maps = []
    for c in range(8):
        b, g = c // 2, c % 2
        heads = [2 * j + g for j in range(HPC)]
        cols = np.concatenate([np.arange(h * 128, (h + 1) * 128) for h in heads])
        wqTa = np.ascontiguousarray(
            Wq.T[:, cols].reshape(NDC, 128, HPC * 128).transpose(1, 0, 2))
        wkTa = np.ascontiguousarray(
            Wk.T[:, cols].reshape(NDC, 128, HPC * 128).transpose(1, 0, 2))
        wvTa = np.ascontiguousarray(
            Wv.T[:, cols].reshape(NDC, 128, HPC * 128).transpose(1, 0, 2))
        woTa = np.ascontiguousarray(Wo.T[cols, :].reshape(HPC, 128, D))
        bqka = np.ascontiguousarray(
            np.stack([bq[cols].reshape(HPC, 128).T,
                      bk[cols].reshape(HPC, 128).T], axis=1))
        bvrowa = np.ascontiguousarray(bv[cols][None, :])
        borowa = np.ascontiguousarray((bo if g == 0 else np.zeros_like(bo))[None, :])

        al = np.zeros((128, HPC, NQT, 32), dtype=f32)
        nsqa = np.zeros((RANK1_SLOTS, S), dtype=f32)
        for j in range(HPC):
            sl = SLOPES[heads[j]]
            for qt in range(NQT):
                anchor = qt * SQ if j < RANK1_SLOTS else (qt + 1) * SQ
                for ck in range(4 * (qt + 1)):
                    kglob = ck * 128 + np.arange(128)
                    al[:, j, qt, ck] = sl * (kglob - anchor)
                if j < RANK1_SLOTS:
                    q = np.arange(qt * SQ, (qt + 1) * SQ)
                    nsqa[j, q] = -sl * (q - qt * SQ) * math.sqrt(Dh)
        meta_a = np.array([[g * OWN, g * NTC]], dtype=np.int32)

        in_maps.append({
            "x_full": x[b],
            "x_own": np.ascontiguousarray(x[b, g * OWN:(g + 1) * OWN]),
            "wqT": wqTa, "wkT": wkTa, "wvT": wvTa, "woT": woTa,
            "bqk": bqka, "bvrow": bvrowa, "borow": borowa,
            "gwT": gwTa,
            "alibi": np.ascontiguousarray(al),
            "nsq": np.ascontiguousarray(nsqa[None, :, :]),
            "dmask": dmaska, "ident": identa, "meta": meta_a,
            "w1s": w1s, "w3s": w3s, "w2T": w2Ta,
            "sw1s": sw1s, "sw3s": sw3s, "sw2T": sw2Ta,
        })
    return in_maps


def kernel(**inputs):
    in_maps = _prep_inputs(inputs)
    nc = build_program()
    res = run_bass_kernel_spmd(nc, in_maps, core_ids=list(range(8)))
    out = np.empty((B, S, D), dtype=np.float32)
    for c in range(8):
        b, g = c // 2, c % 2
        out[b, g * OWN:(g + 1) * OWN] = res.results[c]["out_own"]
    return out
